# version 02
# baseline (speedup 1.0000x reference)
import contextlib
import numpy as np
import ml_dtypes
import concourse.bass as bass
import concourse.mybir as mybir
from concourse.bass_utils import run_bass_kernel_spmd

F32 = mybir.dt.float32
BF16 = mybir.dt.bfloat16
AX = mybir.AxisListType
ALU = mybir.AluOpType
ACT = mybir.ActivationFunctionType

DEBUG_DUMP = False
NPHASES = 4
NRB_LIM = 8
F_PARTS = 7

S, D, H, HD, HF = 4096, 2048, 16, 128, 5632
CH = 512
NCH = S // CH
RB = 512
NRB = S // RB
NCORES = 8
QSCALE = float(1.0 / np.sqrt(128.0))
EPS_LN = 1e-5
EPS_Q = 1e-8
NW = float(HF * D)
NEG = -1e9


def latest(toks):
    d = {}
    for c, v in toks:
        k = id(c)
        if k not in d or d[k][1] < v:
            d[k] = (c, v)
    return list(d.values())


def ops_end(nc, refs, ops, Q_S, Q_C, Q_G, se, emit):
    se(Q_C.tokens(), lambda e: e.nop(), None)
    emit("sync", Q_S.tokens(), lambda e: e.nop(), None)
    emit("gpsimd", Q_G.tokens(), lambda e: e.nop(), None)
    import contextlib as _ctx
    with _ctx.ExitStack() as stack:
        for name, ref in refs.items():
            ref.sem = stack.enter_context(nc.semaphore(name))
        with nc.Block() as block:
            @block.sync
            def _(e):
                for fn in ops["sync"]:
                    fn(e)

            @block.gpsimd
            def _(e):
                for fn in ops["gpsimd"]:
                    fn(e)

            @block.vector
            def _(e):
                for fn in ops["vector"]:
                    fn(e)

            @block.scalar
            def _(e):
                for fn in ops["scalar"]:
                    fn(e)

            @block.tensor
            def _(e):
                for fn in ops["tensor"]:
                    fn(e)


def build_nc(debug=False):
    nc = bass.Bass("TRN2", target_bir_lowering=False)
    scratch_kind = "ExternalOutput" if debug else "Internal"

    x_h = nc.dram_tensor("x", [S, D], F32, kind="ExternalInput").ap()
    win = {w: nc.dram_tensor(w, [D, D], F32, kind="ExternalInput").ap()
           for w in ("wq", "wk", "wv", "wo")}
    w1_h = nc.dram_tensor("w1", [HF, D], F32, kind="ExternalInput").ap()
    w2_h = nc.dram_tensor("w2", [HF, D], F32, kind="ExternalInput").ap()
    w3_h = nc.dram_tensor("w3", [D, HF], F32, kind="ExternalInput").ap()
    ident_h = nc.dram_tensor("ident", [128, 128], BF16, kind="ExternalInput").ap()
    ident4_h = nc.dram_tensor("ident4", [128, 512], BF16, kind="ExternalInput").ap()
    mask4_h = nc.dram_tensor("mask4", [128, 512], F32, kind="ExternalInput").ap()
    out_h = nc.dram_tensor("out", [S, D], F32, kind="ExternalOutput").ap()

    wbf = {w: nc.dram_tensor(f"wbf_{w}", [D, D], BF16, kind="Internal").ap()
           for w in ("q", "k", "v", "o")}
    wT = {w: nc.dram_tensor(f"wT_{w}", [H, 128, D], BF16, kind="Internal").ap()
          for w in ("q", "k", "v")}
    woT_h = nc.dram_tensor("woT", [16, 128, D], BF16, kind="Internal").ap()
    q1_h = nc.dram_tensor("q1", [HF, D], BF16, kind=scratch_kind).ap()
    q2_h = nc.dram_tensor("q2", [HF, D], BF16, kind="Internal").ap()
    q3_h = nc.dram_tensor("q3", [D, HF], BF16, kind="Internal").ap()
    w1T_h = nc.dram_tensor("w1T", [44, 128, 2048], BF16, kind="Internal").ap()
    w2T_h = nc.dram_tensor("w2T", [44, 128, 2048], BF16, kind="Internal").ap()
    w3T_h = nc.dram_tensor("w3T", [4, 128, 44 * 512], BF16, kind="Internal").ap()
    ln1_hbm = nc.dram_tensor("ln1s", [S, D], BF16, kind=scratch_kind).ap()
    A_hbm = nc.dram_tensor("As", [32, 128, 2048], BF16, kind=scratch_kind).ap()
    x2_hbm = nc.dram_tensor("x2s", [S, D], F32, kind=scratch_kind).ap()
    ln3_hbm = nc.dram_tensor("ln3s", [S, D], BF16, kind=scratch_kind).ap()

    NBF = 71936
    NFA = 13824
    BF = nc.alloc_sbuf_tensor("bfar", [128, NBF], BF16,
                              is_transpose_dest=True).ap()
    FA = nc.alloc_sbuf_tensor("f32ar", [128, NFA], F32).ap()
    assert nc.sbuf_bytes_remaining >= 0

    ident = BF[:, 0:128]
    ident4 = BF[:, 128:640]
    mask4 = FA[:, 0:512]
    ones = FA[:, 512:640]
    gtmp = FA[:, 640:768]
    bf0, f0 = 640, 768

    ps = [nc.alloc_psum_tensor(f"ps{i}", [128, 512], F32).ap() for i in range(8)]

    ops = {k: [] for k in ("tensor", "vector", "scalar", "sync", "gpsimd")}
    last_wait = {k: {} for k in ops}

    class SemRef:
        sem = None

    class Ctr:
        def __init__(self, ref, step):
            self.ref, self.n, self.step = ref, 0, step

        @property
        def sem(self):
            return self.ref.sem

        def inc(self):
            self.n += self.step
            return self.n

    NQ = 4
    refs = {}

    def _mkctr(name, step):
        refs[name] = SemRef()
        return Ctr(refs[name], step)

    class Queue:
        def __init__(self, name, n=NQ):
            self.slots = [_mkctr(f"{name}{i}", 16) for i in range(n)]
            self.i = 0

        def next_slot(self):
            s = self.slots[self.i % len(self.slots)]
            self.i += 1
            return s

        def tokens(self):
            return [(s, s.n) for s in self.slots]

    Q_S = Queue("dS")
    Q_C = Queue("dC")
    Q_G = Queue("dG", 2)
    T = _mkctr("T", 1)
    V = _mkctr("V", 1)
    C = _mkctr("C", 1)

    def emit(eng, waits, build, inc=None):
        if eng == "vector":
            waits = list(waits) + [(V, V.n)]
        elif eng == "scalar":
            waits = list(waits) + [(C, C.n)]
        ws = []
        for ctr, val in waits:
            if ctr is None or val <= 0:
                continue
            key = id(ctr)
            if last_wait[eng].get(key, -1) >= val:
                continue
            last_wait[eng][key] = val
            ws.append((ctr, val))
        if inc is not None:
            inc.inc()

        def step(e, ws=ws, build=build, inc=inc):
            for ctr, val in ws:
                e.wait_ge(ctr.sem, val)
            ins = build(e)
            if inc is not None:
                if ins is None:
                    e.nop().then_inc(inc.sem, inc.step)
                else:
                    ins.then_inc(inc.sem, inc.step)
        ops[eng].append(step)

    def te(waits, build, inc=None):
        emit("tensor", waits, build, inc)

    def ve(waits, build, inc=None):
        emit("vector", waits, build, inc)

    def se(waits, build, inc=None):
        emit("scalar", waits, build, inc)

    EMIT = {"vector": ve, "scalar": se}
    ECTR = {"vector": V, "scalar": C}
    QENG = {"sync": Q_S, "scalar": Q_C, "gpsimd": Q_G}

    def dma(eng, waits, build):
        q = QENG[eng]
        slot = q.next_slot()
        emit(eng, list(waits) + [(slot, slot.n)], build, inc=slot)
        return (slot, slot.n)

    def phase_barrier():
        waits = ([(T, T.n), (V, V.n), (C, C.n)] + Q_S.tokens() + Q_C.tokens()
                 + Q_G.tokens())
        own = {"tensor": T, "vector": V, "scalar": C}
        for eng in ops:
            o = own.get(eng)
            ws = [(c, v) for (c, v) in waits if c is not o]
            emit(eng, ws, lambda e: None, None)

    st_sum = gtmp[:, 0:1]
    st_sq = gtmp[:, 1:2]
    st_m = gtmp[:, 2:3]
    st_t1 = gtmp[:, 3:4]
    st_nv = gtmp[:, 4:5]
    st_r = gtmp[:, 5:6]
    sq4 = gtmp[:, 24:28]
    c_eps = gtmp[:, 124:125]
    c_nhalf = gtmp[:, 125:126]
    c_none = gtmp[:, 126:127]
    c_qs = gtmp[:, 127:128]

    from concourse.hw_specs import get_activation_tables
    _tables = list(get_activation_tables(nc.m.arch).keys())

    def act_load(set_name):
        idx = _tables.index(set_name)

        def b(e, idx=idx):
            return e.add_instruction(mybir.InstLoadActFuncSet(
                name=nc.get_next_instruction_name(),
                act_func_set_id=idx, ins=[], outs=[]))
        se([], b, None)

    def emit_ln(src, width, trash, extra_v=(), extra_c=()):
        ve(list(extra_v), lambda e: e.tensor_reduce(
            st_sum, src, AX.X, ALU.add), V)
        nsp = width // 512
        if trash.shape[1] >= width:
            se(list(extra_c), lambda e: e.activation(
                trash, src, ACT.Square, accum_out=st_sq), C)
            sq_c = C.n
            ve([(C, sq_c)], lambda e: e.tensor_scalar(
                st_t1, st_sq, 1.0 / width, None, ALU.mult), V)
        else:
            for i in range(nsp):
                se(list(extra_c) if i == 0 else [],
                   lambda e, i=i: e.activation(
                       trash[:, 0:512], src[:, i * 512:(i + 1) * 512],
                       ACT.Square, accum_out=sq4[:, i:i + 1]), C)
            sq_c = C.n
            ve([(C, sq_c)], lambda e, nsp=nsp: e.tensor_reduce(
                st_sq, sq4[:, 0:nsp], AX.X, ALU.add), V)
            ve([], lambda e: e.tensor_scalar(
                st_t1, st_sq, 1.0 / width, None, ALU.mult), V)
        ve([], lambda e: e.tensor_scalar(st_m, st_sum, 1.0 / width, None,
                                         ALU.mult), V)
        ve([], lambda e: e.scalar_tensor_tensor(
            st_nv, st_m, st_m, st_t1, ALU.mult, ALU.subtract), V)
        se([(V, V.n)], lambda e: e.activation(
            st_t1, st_nv, ACT.Ln, bias=c_eps, scale=c_none), C)
        se([], lambda e: e.activation(st_r, st_t1, ACT.Exp, scale=c_nhalf), C)
        return C.n

    consts_toks = [
        dma("sync", [], lambda e: e.dma_start(ident, ident_h)),
        dma("sync", [], lambda e: e.dma_start(ident4, ident4_h)),
        dma("sync", [], lambda e: e.dma_start(mask4, mask4_h)),
    ]
    act_load("natural_log_exp_and_others")
    ve([], lambda e: e.memset(ones, 1.0), V)
    ve([], lambda e: e.memset(c_eps, EPS_LN), V)
    ve([], lambda e: e.memset(c_nhalf, -0.5), V)
    ve([], lambda e: e.memset(c_none, -1.0), V)
    ve([], lambda e: e.memset(c_qs, QSCALE), V)
    ones_done = V.n

    wload = [FA[:, f0 + i * 5632: f0 + (i + 1) * 5632] for i in range(2)]
    p_b = bf0
    qstage = [BF[:, p_b + i * 5632: p_b + (i + 1) * 5632] for i in range(2)]
    p_b += 2 * 5632
    lbuf = BF[:, p_b: p_b + 5632]
    p_b += 5632
    tslab = [BF[:, p_b + i * 5632: p_b + (i + 1) * 5632] for i in range(2)]
    p_b += 2 * 5632

    cast_toks = [dma("gpsimd", [],
                     lambda e, a=wbf[w], b=win["w" + w]: e.dma_start(a, b))
                 for w in ("q", "k", "v", "o")]

    ring_cons = [0, 0]
    for wi, (srcw, nt, wdt) in enumerate(((w1_h, 44, 2048), (w2_h, 44, 2048),
                                          (w3_h, 16, 5632))):
        for t in range(nt):
            b = t % 2
            tok = dma("sync", [(V, ring_cons[b])],
                      lambda e, d=wload[b], s=srcw, t=t, wdt=wdt: e.dma_start(
                          d[:, 0:wdt], s[t * 128:(t + 1) * 128, 0:wdt]))
            ve([tok],
               lambda e, sb=wload[b], wdt=wdt, gc=32 + t: e.tensor_reduce(
                   gtmp[:, gc:gc + 1], sb[:, 0:wdt], AX.X, ALU.add,
                   apply_absolute_value=True), V)
            ring_cons[b] = V.n
        ve([], lambda e, nt=nt, wi=wi: e.tensor_reduce(
            gtmp[:, 28 + wi:29 + wi], gtmp[:, 32:32 + nt], AX.X, ALU.add), V)
    acc3_done = V.n
    te([(V, acc3_done), (V, ones_done)],
       lambda e: e.matmul(ps[0][0:1, 0:3], ones[:, 0:1], gtmp[:, 28:31],
                          start=True, stop=True), T)
    thr_row = gtmp[0:1, 120:123]
    ve([(T, T.n)],
       lambda e: e.tensor_scalar(thr_row, ps[0][0:1, 0:3], 0.5 / NW,
                                 0.5 * EPS_Q, ALU.mult, ALU.add), V)
    te([(V, V.n)],
       lambda e: e.matmul(ps[0][0:128, 4:7], ones[0:1, 0:128], thr_row,
                          start=True, stop=True), T)
    thr = gtmp[:, 116:119]
    thrn = gtmp[:, 112:115]
    ve([(T, T.n)], lambda e: e.tensor_copy(thr, ps[0][:, 4:7]), V)
    ve([], lambda e: e.tensor_scalar(thrn, thr, -1.0, None, ALU.mult), V)
    thr_done = V.n

    st_cons = [None, None]
    q_store_toks = {0: [], 1: [], 2: []}
    for wi, (srcw, dstw, nt, wdt) in enumerate(((w1_h, q1_h, 44, 2048),
                                                (w2_h, q2_h, 44, 2048),
                                                (w3_h, q3_h, 16, 5632))):
        for t in range(nt):
            b = t % 2
            tok = dma("sync", [(V, ring_cons[b])],
                      lambda e, d=wload[b], s=srcw, t=t, wdt=wdt: e.dma_start(
                          d[:, 0:wdt], s[t * 128:(t + 1) * 128, 0:wdt]))
            ve([tok, (V, thr_done)] + ([st_cons[b]] if st_cons[b] else []),
               lambda e, s=wload[b], wdt=wdt, wi=wi: e.tensor_scalar(
                   lbuf[:, 0:wdt], s[:, 0:wdt], thrn[:, wi:wi + 1], None,
                   ALU.is_le), V)
            ve([],
               lambda e, s=wload[b], q=qstage[b], wdt=wdt, wi=wi:
               e.scalar_tensor_tensor(q[:, 0:wdt], s[:, 0:wdt],
                                      thr[:, wi:wi + 1], lbuf[:, 0:wdt],
                                      ALU.is_ge, ALU.subtract), V)
            ring_cons[b] = V.n
            stok = dma("scalar", [(V, V.n)],
                       lambda e, q=qstage[b], d=dstw, t=t, wdt=wdt:
                       e.dma_start(d[t * 128:(t + 1) * 128, 0:wdt],
                                   q[:, 0:wdt]))
            st_cons[b] = stok
            q_store_toks[wi].append(stok)
    q_gate = {wi: latest(q_store_toks[wi]) for wi in range(3)}

    tr_cons = [None, None]
    ti = 0
    for w in ("q", "k", "v", "o"):
        for Dc in range(16):
            b = ti % 2
            ti += 1
            ttok = dma("sync",
                       cast_toks + ([tr_cons[b]] if tr_cons[b] else []),
                       lambda e, d=tslab[b], s=wbf[w], Dc=Dc: e.dma_start(
                           d[:, 0:2048], s[:, Dc * 128:(Dc + 1) * 128],
                           transpose=True))
            if w == "o":
                tr_cons[b] = dma("scalar", [ttok],
                                 lambda e, s=tslab[b], Dc=Dc: e.dma_start(
                                     woT_h[Dc], s[:, 0:2048]))
            else:
                tr_cons[b] = dma(
                    "scalar", [ttok],
                    lambda e, s=tslab[b], w=w, Dc=Dc: e.dma_start(
                        wT[w][:, :, Dc * 128:(Dc + 1) * 128].rearrange(
                            "h p c -> p h c"),
                        s[:, 0:2048].rearrange("p (h c) -> p h c", h=16)))

    for qi, (qh, wTd) in enumerate(((q1_h, w1T_h), (q2_h, w2T_h))):
        for Dc in range(16):
            b = ti % 2
            ti += 1
            ttok = dma("sync",
                       q_gate[qi] + ([tr_cons[b]] if tr_cons[b] else []),
                       lambda e, d=tslab[b], s=qh, Dc=Dc: e.dma_start(
                           d[:, 0:5632], s[:, Dc * 128:(Dc + 1) * 128],
                           transpose=True))
            tr_cons[b] = dma(
                "scalar", [ttok],
                lambda e, s=tslab[b], d=wTd, Dc=Dc: e.dma_start(
                    d[:, :, Dc * 128:(Dc + 1) * 128].rearrange(
                        "k p c -> p k c"),
                    s[:, 0:5632].rearrange("p (k c) -> p k c", k=44)))
    for cc in range(44):
        b = ti % 2
        ti += 1
        ttok = dma("sync", q_gate[2] + ([tr_cons[b]] if tr_cons[b] else []),
                   lambda e, d=tslab[b], cc=cc: e.dma_start(
                       d[:, 0:2048], q3_h[:, cc * 128:(cc + 1) * 128],
                       transpose=True))
        tr_cons[b] = dma(
            "scalar", [ttok],
            lambda e, s=tslab[b], cc=cc: e.dma_start(
                w3T_h[:, :, cc * 512:(cc + 1) * 512].rearrange(
                    "k p c -> p k c"),
                s[:, 0:2048].rearrange("p (k c) -> p k c", k=4)))

    phase_barrier()
    if NPHASES < 2:
        ops_end(nc, refs, ops, Q_S, Q_C, Q_G, se, emit)
        return nc
    p_b = bf0
    ln1b = [BF[:, p_b + i * 2048: p_b + (i + 1) * 2048] for i in range(2)]
    p_b += 4096
    ln1T = [BF[:, p_b + i * 8192: p_b + (i + 1) * 8192] for i in range(2)]
    p_b += 16384
    wstream = [BF[:, p_b + i * 2048: p_b + (i + 1) * 2048] for i in range(3)]
    p_b += 6144
    QT = BF[:, p_b: p_b + 8192]; p_b += 8192
    KT = BF[:, p_b: p_b + 8192]; p_b += 8192
    VT = BF[:, p_b: p_b + 8192]; p_b += 8192
    Abuf = [BF[:, p_b + i * 8192: p_b + (i + 1) * 8192] for i in range(2)]
    p_b += 16384
    expS = [BF[:, p_b + i * 512: p_b + (i + 1) * 512] for i in range(2)]
    p_b += 1024
    attnT_sb = [BF[:, p_b + i * 512: p_b + (i + 1) * 512] for i in range(2)]
    p_b += 1024
    diag4 = [BF[:, p_b + i * 512: p_b + (i + 1) * 512] for i in range(2)]
    p_b += 1024
    assert p_b <= NBF, p_b

    xbuf = [FA[:, f0 + i * 2048: f0 + (i + 1) * 2048] for i in range(2)]
    smask = [FA[:, f0 + 4096 + i * 512: f0 + 4096 + (i + 1) * 512]
             for i in range(2)]
    trashA = FA[:, f0 + 5120: f0 + 7168]
    nmx4 = [gtmp[:, 8:12], gtmp[:, 56:60]]
    sums4 = [gtmp[:, 12:16], gtmp[:, 60:64]]
    rec4 = [gtmp[:, 16:20], gtmp[:, 64:68]]

    x_cons = [0, 0]
    ln1b_cons = [None, None]
    ln1T_cons = [0, 0]
    wst_cons = [0, 0, 0]
    A_cons = [None, None]
    bank_free = {i: None for i in range(8)}
    last_exp = [0, 0]
    last_aT_mm = [0, 0]
    last_out_mm = [0, 0]
    qkv_evac = {"q": "scalar", "k": "vector", "v": "vector"}
    ln1T_ready = [None, None]

    def emit_ln1_chunk(c):
        pos0 = c * CH
        ln1_store_toks = []
        for i in range(4):
            xb = xbuf[i % 2]
            r0 = pos0 + i * 128
            xtok = dma("sync", [(V, x_cons[i % 2])],
                       lambda e, d=xb, r0=r0: e.dma_start(
                           d, x_h[r0:r0 + 128, :]))
            r_done = emit_ln(xb, 2048, trashA, extra_v=[xtok],
                             extra_c=[xtok])
            lb = ln1b[i % 2]
            ve([(C, r_done)] + ([ln1b_cons[i % 2]] if ln1b_cons[i % 2]
                                else []),
               lambda e, s=xb, d=lb: e.tensor_scalar(
                   d, s, st_m, st_r, ALU.subtract, ALU.mult), V)
            x_cons[i % 2] = V.n
            stok = dma("scalar", [(V, V.n)],
                       lambda e, s=lb, r0=r0: e.dma_start(
                           ln1_hbm[r0:r0 + 128, :], s))
            ln1b_cons[i % 2] = stok
            ln1_store_toks.append(stok)
        ln1_gate = latest(ln1_store_toks)
        par = c % 2
        lt = ln1T[par]
        lnT_toks = []
        for Dc in range(16):
            lnT_toks.append(dma(
                "sync", ln1_gate + [(T, ln1T_cons[par])],
                lambda e, d=lt, Dc=Dc, p0=pos0: e.dma_start(
                    d[:, Dc * 512:(Dc + 1) * 512],
                    ln1_hbm[p0:p0 + CH, Dc * 128:(Dc + 1) * 128],
                    transpose=True)))
        ln1T_ready[par] = latest(lnT_toks)

    emit_ln1_chunk(0)
    for c in range(NCH):
        par = c % 2
        lt = ln1T[par]

        for pi, p in enumerate(("q", "k", "v")):
            ename = qkv_evac[p]
            for h in range(H):
                gi = pi * H + h
                rbuf = gi % 3
                wtok = dma("sync", [(T, wst_cons[rbuf])],
                           lambda e, d=wstream[rbuf], p=p, h=h: e.dma_start(
                               d, wT[p][h]))
                bank = gi % 2
                tw = [wtok] + ln1T_ready[par] + (bank_free[bank] or [])
                for Dc in range(16):
                    te(tw if Dc == 0 else [],
                       lambda e, b=bank, wl=wstream[rbuf], lt=lt, Dc=Dc:
                       e.matmul(ps[b][:], wl[:, Dc * 128:(Dc + 1) * 128],
                                lt[:, Dc * 512:(Dc + 1) * 512],
                                start=(Dc == 0), stop=(Dc == 15)),
                       T if Dc == 15 else None)
                wst_cons[rbuf] = T.n
                mm_done = T.n
                if p == "q":
                    dst = QT.rearrange("p (g s h) -> p g s h", g=64, s=8)[
                        :, :, :, h]
                    srcp = ps[bank][:].rearrange("p (g s) -> p g s", g=64)
                    se([(T, mm_done)],
                       lambda e, d=dst, s=srcp: e.activation(
                           d, s, ACT.Copy, scale=c_qs), C)
                else:
                    tgt = KT if p == "k" else VT
                    dst = tgt.rearrange("p (g t s) -> p g t s", g=64, t=16)[
                        :, :, h, :]
                    srcp = ps[bank][:].rearrange("p (g s) -> p g s", g=64)
                    ve([(T, mm_done)],
                       lambda e, d=dst, s=srcp: e.tensor_copy(d, s), V)
                bank_free[bank] = [(ECTR[ename], ECTR[ename].n)]
        ln1T_cons[par] = T.n
        qkv_done = [(V, V.n), (C, C.n)]

        for g4 in range(16):
            bank = 2 + (g4 % 2)
            tw = qkv_done + (bank_free[bank] or []) + consts_toks
            for i in range(4):
                g8 = 4 * g4 + i
                te(tw if i == 0 else [],
                   lambda e, b=bank, i=i, g8=g8: e.matmul(
                       ps[b][:, i * 128:(i + 1) * 128],
                       VT[:, g8 * 128:(g8 + 1) * 128], ident,
                       start=True, stop=True),
                   T if i == 3 else None)
            ve([(T, T.n)], lambda e, b=bank, g4=g4: e.tensor_copy(
                VT[:, g4 * 512:(g4 + 1) * 512], ps[b][:]), V)
            bank_free[bank] = [(V, V.n)]
        vgrp_done = V.n

        if c + 1 < NCH:
            emit_ln1_chunk(c + 1)

        for g4 in range(16):
            p4 = g4 % 2
            sb = 4 if p4 == 0 else 7
            aTb = 5 if p4 == 0 else 3
            ob = 6 if p4 == 0 else 2
            smaskp = smask[p4]
            expSp = expS[p4]
            attnTp = attnT_sb[p4]
            diagp = diag4[p4]
            nmx4p = nmx4[p4]
            sums4p = sums4[p4]
            rec4p = rec4[p4]
            tw = [(V, vgrp_done)] + (bank_free[sb] or [])
            for i in range(4):
                g8 = 4 * g4 + i
                te(tw if i == 0 else [],
                   lambda e, sb=sb, i=i, g8=g8: e.matmul(
                       ps[sb][:, i * 128:(i + 1) * 128],
                       QT[:, g8 * 128:(g8 + 1) * 128],
                       KT[:, g8 * 128:(g8 + 1) * 128],
                       start=True, stop=True),
                   T if i == 3 else None)
            sc_mm = T.n
            ve([(T, sc_mm), (C, last_exp[p4])] + consts_toks,
               lambda e, sb=sb, d=smaskp: e.tensor_tensor(
                   d, ps[sb][:], mask4, ALU.add), V)
            bank_free[sb] = [(V, V.n)]
            ve([], lambda e, s=smaskp, d=nmx4p: e.tensor_reduce(
                d, s.rearrange("p (g cc) -> p g cc", g=4), AX.X,
                ALU.max, negate=True), V)
            mask_done = V.n
            for i in range(4):
                se([(V, mask_done), (T, last_aT_mm[p4])],
                   lambda e, i=i, s=smaskp, d=expSp, n=nmx4p, m=sums4p:
                   e.activation(
                       d[:, i * 128:(i + 1) * 128],
                       s[:, i * 128:(i + 1) * 128], ACT.Exp,
                       bias=n[:, i:i + 1],
                       accum_out=m[:, i:i + 1]), C)
            last_exp[p4] = C.n
            ve([(C, last_exp[p4])],
               lambda e, s=sums4p, d=rec4p: e.reciprocal(d, s), V)
            ve([(T, last_aT_mm[p4])],
               lambda e, d=diagp, r=rec4p: e.tensor_tensor(
                   d.rearrange("p (g cc) -> p g cc", g=4),
                   ident4.rearrange("p (g cc) -> p g cc", g=4),
                   r.broadcast_to([128, 4, 128]), ALU.mult), V)
            diag_done = V.n
            tw = [(V, diag_done)] + (bank_free[aTb] or [])
            for i in range(4):
                te(tw if i == 0 else [],
                   lambda e, i=i, b=aTb, s=expSp, d=diagp: e.matmul(
                       ps[b][:, i * 128:(i + 1) * 128],
                       s[:, i * 128:(i + 1) * 128],
                       d[:, i * 128:(i + 1) * 128],
                       start=True, stop=True),
                   T if i == 3 else None)
            last_aT_mm[p4] = T.n
            se([(T, last_aT_mm[p4]), (T, last_out_mm[p4])],
               lambda e, b=aTb, d=attnTp: e.copy(d, ps[b][:]), C)
            bank_free[aTb] = [(C, C.n)]
            aT_evac = C.n
            tw = [(C, aT_evac)] + (bank_free[ob] or [])
            for i in range(4):
                g8 = 4 * g4 + i
                te(tw if i == 0 else [],
                   lambda e, i=i, g8=g8, b=ob, s=attnTp: e.matmul(
                       ps[b][:, i * 128:(i + 1) * 128],
                       VT[:, g8 * 128:(g8 + 1) * 128],
                       s[:, i * 128:(i + 1) * 128],
                       start=True, stop=True),
                   T if i == 3 else None)
            last_out_mm[p4] = T.n
            Ab = Abuf[c % 2]
            for parx in range(2):
                srcA = ps[ob][:].rearrange("p (i a s h) -> p i a s h",
                                           i=2, a=2, s=8)[:, :, parx, :, :]
                dstA = Ab.rearrange("p (a s g h) -> p a s g h",
                                    a=2, s=8, g=32)[
                    :, parx, :, 2 * g4: 2 * g4 + 2, :].rearrange(
                    "p s g h -> p g s h")
                ve([(T, last_out_mm[p4])] + ([A_cons[c % 2]]
                                             if A_cons[c % 2] else []),
                   lambda e, s=srcA, d=dstA: e.tensor_copy(d, s), V)
            bank_free[ob] = [(V, V.n)]
            A_evac = V.n
        A_cons[c % 2] = dma(
            "scalar", [(V, A_evac)],
            lambda e, s=Abuf[c % 2], c=c: e.dma_start(
                A_hbm[4 * c: 4 * c + 4].rearrange("b p (j cc) -> p j b cc",
                                                  j=16),
                s.rearrange("p (j b cc) -> p j b cc", j=16, b=4)))

    phase_barrier()
    if NPHASES < 3:
        ops_end(nc, refs, ops, Q_S, Q_C, Q_G, se, emit)
        return nc
    p_b = bf0
    woT_sb = BF[:, p_b: p_b + 16 * 2048]; p_b += 32768
    Aslab = [BF[:, p_b + i * 2048: p_b + (i + 1) * 2048] for i in range(2)]
    p_b += 4096
    ln3b = [BF[:, p_b + i * 2048: p_b + (i + 1) * 2048] for i in range(2)]
    p_b += 4096
    assert p_b <= NBF

    Ost = [FA[:, f0 + i * 2048: f0 + (i + 1) * 2048] for i in range(2)]
    xg = [FA[:, f0 + 4096 + i * 2048: f0 + 4096 + (i + 1) * 2048]
          for i in range(2)]
    x2b = [FA[:, f0 + 8192 + i * 2048: f0 + 8192 + (i + 1) * 2048]
           for i in range(2)]
    trashO = FA[:, f0 + 12288: f0 + 12800]

    woT_ready = [dma("sync", [],
                     lambda e, j0=j0: e.dma_start(
                         woT_sb[:, j0 * 2048:(j0 + 4) * 2048].rearrange(
                             "p (j cc) -> p j cc", j=4),
                         woT_h[j0:j0 + 4].rearrange("j p cc -> p j cc")))
                 for j0 in range(0, 16, 4)]

    Aslab_cons = [0, 0]
    xg_cons = [0, 0]
    Ost_cons = [0, 0]
    x2_cons = [0, 0]
    ln3_store_tok = [None] * 32
    x2_store_tok = [None] * 32
    ln3b_cons = [None, None]
    for cb in range(32):
        par = cb % 2
        A_ld = dma("sync", [(T, Aslab_cons[par])],
                   lambda e, d=Aslab[par], cb=cb: e.dma_start(d, A_hbm[cb]))
        xg_ld = dma("gpsimd", [(V, xg_cons[par])],
                    lambda e, d=xg[par], cb=cb: e.dma_start(
                        d, x_h.rearrange("(h g) dd -> g h dd", h=16)[
                            8 * cb: 8 * cb + 8]))
        for dc in range(4):
            bank = dc
            tw = [A_ld] + woT_ready + (bank_free[bank] or [])
            for j in range(16):
                te(tw if j == 0 else [],
                   lambda e, b=bank, j=j, dc=dc, s=Aslab[par]: e.matmul(
                       ps[b][:], s[:, j * 128:(j + 1) * 128],
                       woT_sb[:, j * 2048 + dc * 512:
                              j * 2048 + (dc + 1) * 512],
                       start=(j == 0), stop=(j == 15)),
                   T if j == 15 else None)
            mm = T.n
            eng = "scalar" if dc % 2 == 0 else "vector"
            EMIT[eng]([(T, mm), (V, Ost_cons[par])],
                      lambda e, b=bank, dc=dc, d=Ost[par], eng=eng:
                      (e.copy if eng == "scalar" else e.tensor_copy)(
                          d[:, dc * 512:(dc + 1) * 512], ps[b][:]),
                      ECTR[eng])
            bank_free[bank] = [(ECTR[eng], ECTR[eng].n)]
        o_evac = [(V, V.n), (C, C.n)]
        r2 = emit_ln(Ost[par], 2048, trashO, extra_v=o_evac, extra_c=o_evac)
        ve([(C, r2), xg_ld, (V, x2_cons[par])],
           lambda e, s=Ost[par]: e.tensor_scalar(
               s, s, st_m, st_r, ALU.subtract, ALU.mult), V)
        ve(([x2_store_tok[cb - 2]] if cb >= 2 else []),
           lambda e, s=Ost[par], x=xg[par], d=x2b[par]: e.tensor_tensor(
               d, s, x, ALU.add), V)
        Ost_cons[par] = V.n
        xg_cons[par] = V.n
        x2_done = V.n
        x2_store_tok[cb] = dma("scalar", [(V, x2_done)],
                               lambda e, s=x2b[par], cb=cb: e.dma_start(
                                   x2_hbm[cb * 128:(cb + 1) * 128, :], s))
        r3 = emit_ln(x2b[par], 2048, trashO,
                     extra_v=[(V, x2_done)], extra_c=[(V, x2_done)])
        ve([(C, r3)] + ([ln3b_cons[par]] if ln3b_cons[par] else []),
           lambda e, s=x2b[par], d=ln3b[par]: e.tensor_scalar(
               d, s, st_m, st_r, ALU.subtract, ALU.mult), V)
        x2_cons[par] = V.n
        ln3_store_tok[cb] = dma("scalar", [(V, V.n)],
                                lambda e, s=ln3b[par], cb=cb: e.dma_start(
                                    ln3_hbm[cb * 128:(cb + 1) * 128, :], s))
        ln3b_cons[par] = ln3_store_tok[cb]
        Aslab_cons[par] = T.n

    phase_barrier()
    if NPHASES < 4:
        ops_end(nc, refs, ops, Q_S, Q_C, Q_G, se, emit)
        return nc
    p_b = bf0
    ln3T = BF[:, p_b: p_b + 8192]; p_b += 8192
    hid = BF[:, p_b: p_b + 44 * 512]; p_b += 44 * 512
    w12s = [BF[:, p_b + i * 2048: p_b + (i + 1) * 2048] for i in range(4)]
    p_b += 8192
    w3h = [BF[:, p_b + i * 22 * 512: p_b + (i + 1) * 22 * 512]
           for i in range(2)]
    p_b += 44 * 512
    fstage = BF[:, p_b: p_b + 4 * 2048]; p_b += 8192
    assert p_b <= NBF, p_b

    sigb = FA[:, f0: f0 + 512]
    x2f = [FA[:, f0 + 512 + i * 2048: f0 + 512 + (i + 1) * 2048]
           for i in range(2)]
    outb = [FA[:, f0 + 4608 + i * 2048: f0 + 4608 + (i + 1) * 2048]
            for i in range(2)]
    trashF = FA[:, f0 + 8704: f0 + 9216]

    ln3T_cons = 0
    w12_cons = [0, 0, 0, 0]
    w3h_cons = [0, 0]
    sig_cons = 0
    x2f_cons = [0, 0]
    outb_cons = [None, None]
    fst_cons = [0, 0, 0, 0]
    ln4_prev_V = 0
    for rb in range(min(NRB, NRB_LIM)):
        r0 = rb * RB
        lnT_toks = []
        gate = latest(list(ln3_store_tok[4 * rb: 4 * rb + 4]))
        for Dc in range(16):
            lnT_toks.append(dma(
                "sync", gate + [(T, ln3T_cons)],
                lambda e, Dc=Dc, r0=r0: e.dma_start(
                    ln3T[:, Dc * 512:(Dc + 1) * 512],
                    ln3_hbm[r0:r0 + RB, Dc * 128:(Dc + 1) * 128],
                    transpose=True)))
        ln3T_rdy = latest(lnT_toks)
        act_load("sigmoid_and_others")
        for cp in range(44 if (F_PARTS & 1) else 0):
            rb1 = (2 * cp) % 4
            rb2 = (2 * cp + 1) % 4
            w1_ld = dma("sync", [(T, w12_cons[rb1])],
                        lambda e, d=w12s[rb1], cp=cp: e.dma_start(
                            d, w1T_h[cp]))
            w2_ld = dma("sync", [(T, w12_cons[rb2])],
                        lambda e, d=w12s[rb2], cp=cp: e.dma_start(
                            d, w2T_h[cp]))
            b1 = cp % 2
            b2 = 2 + cp % 2
            tw = [w1_ld] + ln3T_rdy + (bank_free[b1] or [])
            for Dc in range(16):
                te(tw if Dc == 0 else [],
                   lambda e, b=b1, w=w12s[rb1], Dc=Dc: e.matmul(
                       ps[b][:], w[:, Dc * 128:(Dc + 1) * 128],
                       ln3T[:, Dc * 512:(Dc + 1) * 512],
                       start=(Dc == 0), stop=(Dc == 15)),
                   T if Dc == 15 else None)
            x1_mm = T.n
            tw = [w2_ld] + (bank_free[b2] or [])
            for Dc in range(16):
                te(tw if Dc == 0 else [],
                   lambda e, b=b2, w=w12s[rb2], Dc=Dc: e.matmul(
                       ps[b][:], w[:, Dc * 128:(Dc + 1) * 128],
                       ln3T[:, Dc * 512:(Dc + 1) * 512],
                       start=(Dc == 0), stop=(Dc == 15)),
                   T if Dc == 15 else None)
            x2_mm = T.n
            w12_cons[rb1] = x1_mm
            w12_cons[rb2] = x2_mm
            se([(T, x1_mm), (V, sig_cons)],
               lambda e, b=b1: e.activation(sigb, ps[b][:], ACT.Sigmoid), C)
            bank_free[b1] = [(C, C.n)]
            ve([(C, C.n), (T, x2_mm)],
               lambda e, b=b2, cp=cp: e.tensor_tensor(
                   hid[:, cp * 512:(cp + 1) * 512], ps[b][:], sigb,
                   ALU.mult), V)
            sig_cons = V.n
            bank_free[b2] = [(V, V.n)]
        hid_done = V.n
        ln3T_cons = T.n

        for dc in range(4 if (F_PARTS & 2) else 0):
            w3_lds = []
            for half in range(2):
                w3_lds.append(dma(
                    "sync", [(T, w3h_cons[half])],
                    lambda e, dc=dc, half=half: e.dma_start(
                        w3h[half].rearrange("p (k cc) -> p k cc", k=22),
                        w3T_h[dc][:, half * 22 * 512:(half + 1) * 22 * 512]
                        .rearrange("p (k cc) -> p k cc", k=22))))
            for posb in range(4):
                bank = 4 + posb
                tw = [w3_lds[0], (V, hid_done)] + (bank_free[bank] or [])
                for cc in range(44):
                    half, ccl = divmod(cc, 22)
                    tww = tw if cc == 0 else ([w3_lds[1]] if cc == 22 else [])
                    do_inc = (cc == 43) or (cc == 21 and posb == 3)
                    te(tww,
                       lambda e, b=bank, ccl=ccl, half=half, cc=cc, posb=posb:
                       e.matmul(
                           ps[b][:],
                           hid[:, cc * 512 + posb * 128:
                               cc * 512 + (posb + 1) * 128],
                           w3h[half][:, ccl * 512:(ccl + 1) * 512],
                           start=(cc == 0), stop=(cc == 43)),
                       T if do_inc else None)
                    if cc == 21 and posb == 3:
                        w3h_cons[0] = T.n
                mm = T.n
                if posb == 3:
                    w3h_cons[1] = T.n
                eng = "scalar" if posb % 2 == 0 else "vector"
                EMIT[eng]([(T, mm), (V, fst_cons[posb])],
                          lambda e, b=bank, posb=posb, dc=dc, eng=eng:
                          (e.copy if eng == "scalar" else e.tensor_copy)(
                              fstage[:, posb * 2048 + dc * 512:
                                     posb * 2048 + (dc + 1) * 512],
                              ps[b][:]), ECTR[eng])
                bank_free[bank] = [(ECTR[eng], ECTR[eng].n)]
                evac_tok = (ECTR[eng], ECTR[eng].n)
                scol = 32 + posb * 4 + dc
                qcol = 48 + posb * 4 + dc
                fsl = fstage[:, posb * 2048 + dc * 512:
                             posb * 2048 + (dc + 1) * 512]
                ve([evac_tok, (V, ln4_prev_V)],
                   lambda e, s=fsl, scol=scol: e.tensor_reduce(
                       gtmp[:, scol:scol + 1], s, AX.X, ALU.add), V)
                se([evac_tok, (V, ln4_prev_V)],
                   lambda e, s=fsl, qcol=qcol: e.activation(
                       trashF, s, ACT.Square,
                       accum_out=gtmp[:, qcol:qcol + 1]), C)
        fst_done = [(V, V.n), (C, C.n)]

        act_load("natural_log_exp_and_others")
        for posb in range(4 if (F_PARTS & 4) else 0):
            par = posb % 2
            r0p = r0 + posb * 128
            cb = r0p // 128
            x2_ld = dma("gpsimd",
                        [(V, x2f_cons[par]), x2_store_tok[cb]],
                        lambda e, d=x2f[par], r0p=r0p: e.dma_start(
                            d, x2_hbm[r0p:r0p + 128, :]))
            ve(fst_done, lambda e, posb=posb: e.tensor_reduce(
                st_sum, gtmp[:, 32 + posb * 4: 36 + posb * 4],
                AX.X, ALU.add), V)
            ve([], lambda e, posb=posb: e.tensor_reduce(
                st_sq, gtmp[:, 48 + posb * 4: 52 + posb * 4],
                AX.X, ALU.add), V)
            ve([], lambda e: e.tensor_scalar(st_m, st_sum, 1.0 / D, None,
                                             ALU.mult), V)
            ve([], lambda e: e.tensor_scalar(st_t1, st_sq, 1.0 / D, None,
                                             ALU.mult), V)
            ve([], lambda e: e.scalar_tensor_tensor(
                st_nv, st_m, st_m, st_t1, ALU.mult, ALU.subtract), V)
            se([(V, V.n)], lambda e: e.activation(
                st_t1, st_nv, ACT.Ln, bias=c_eps, scale=c_none), C)
            se([], lambda e: e.activation(st_r, st_t1, ACT.Exp,
                                          scale=c_nhalf), C)
            r4 = C.n
            ve([(C, r4)] + ([outb_cons[par]] if outb_cons[par] else []),
               lambda e, posb=posb, d=outb[par]: e.tensor_scalar(
                   d, fstage[:, posb * 2048:(posb + 1) * 2048],
                   st_m, st_r, ALU.subtract, ALU.mult), V)
            fst_cons[posb] = V.n
            ve([x2_ld],
               lambda e, d=outb[par], x=x2f[par]: e.tensor_tensor(
                   d, d, x, ALU.add), V)
            x2f_cons[par] = V.n
            outb_cons[par] = dma(
                "scalar", [(V, V.n)],
                lambda e, s=outb[par], cb=cb: e.dma_start(
                    out_h.rearrange("(h g) dd -> g h dd", h=16)[
                        8 * cb: 8 * cb + 8],
                    s))
        ln4_prev_V = V.n

    ops_end(nc, refs, ops, Q_S, Q_C, Q_G, se, emit)
    return nc


def _consts():
    eye = np.eye(128, dtype=ml_dtypes.bfloat16)
    ident4 = np.concatenate([eye] * 4, axis=1)
    p = np.arange(128)[:, None]
    cc = np.arange(128)[None, :]
    m = np.where((cc % 8) == (p // 16), 0.0, NEG).astype(np.float32)
    mask4 = np.concatenate([m] * 4, axis=1)
    return eye, ident4, mask4


def kernel(L0_attn_x, L0_attn_wq, L0_attn_wk, L0_attn_wv, L0_attn_wo,
           L0_ffn_w1, L0_ffn_w2, L0_ffn_w3):
    x = np.ascontiguousarray(np.asarray(L0_attn_x, dtype=np.float32))
    ws = {n: np.ascontiguousarray(np.asarray(a, dtype=np.float32))
          for n, a in (("wq", L0_attn_wq), ("wk", L0_attn_wk),
                       ("wv", L0_attn_wv), ("wo", L0_attn_wo),
                       ("w1", L0_ffn_w1), ("w2", L0_ffn_w2),
                       ("w3", L0_ffn_w3))}
    eye, ident4, mask4 = _consts()

    nc = build_nc(debug=DEBUG_DUMP)
    in_maps = []
    for b in range(NCORES):
        m = {"x": np.ascontiguousarray(x[b]), "ident": eye,
             "ident4": ident4, "mask4": mask4}
        m.update(ws)
        in_maps.append(m)
    res = run_bass_kernel_spmd(nc, in_maps, core_ids=list(range(NCORES)))
    out = np.stack([np.asarray(res.results[b]["out"], dtype=np.float32)
                    for b in range(NCORES)], axis=0)
    if DEBUG_DUMP:
        kernel._debug = res
    return out
